# version 01
# baseline (speedup 1.0000x reference)
import numpy as np
import ml_dtypes
import concourse.bass as bass
import concourse.mybir as mybir
from concourse.bass_utils import run_bass_kernel_spmd

F32 = mybir.dt.float32
BF16 = mybir.dt.bfloat16
AF = mybir.ActivationFunctionType
ALU = mybir.AluOpType
AX = mybir.AxisListType

D = 1024
SEQ = 2048
BATCH = 32
NCORE = 8
BPC = BATCH // NCORE
NT_SEQ = SEQ // 128
GROUP = 4
NGRP_SEQ = NT_SEQ // GROUP
KCH = 8
PCOLS = 1072

BF = ml_dtypes.bfloat16


class Planner:

    ENGS = ("t", "v", "s", "g", "y")

    def __init__(self):
        self.q = {e: [] for e in self.ENGS}
        self.cnt = {}
        self.gseq = []
        self.gl_total = 0
        self.serial_engines = set()
        self.serial_phases = set()
        self.phase = "setup" 

    def op(self, e, fn, inc=None, delta=1, waits=(), drain=True):
        if e in self.serial_engines and (not self.serial_phases or self.phase in self.serial_phases):
            gidx = self.gl_total
            self.gl_total += 1
            self.q[e].append(("g", gidx))
        for tok in waits:
            if tok is not None:
                self.q[e].append(("w", tok[0], tok[1]))
        if drain and e in ("v", "s", "g"):
            self.q[e].append(("d",))
        tok = None
        val = None
        if inc is not None:
            self.cnt[inc] = self.cnt.get(inc, 0) + delta
            val = self.cnt[inc]
            tok = (inc, val)
        if inc is not None and (e == "t" or (drain and e in ("v", "s", "g"))):
            self.q[e].append(("o", fn, None, None, None))
            import os as _os3
            if _os3.environ.get("HARDEN_V") == "1" and e in ("v", "s", "t", "g"):
                self.q[e].append(("o", lambda eng: eng.drain(), None, None, None))
            self.q[e].append(("o", lambda eng: eng.drain(), inc, delta, val))
        else:
            self.q[e].append(("o", fn, inc, delta, val))
        return tok

    def dma(self, e, fn, inc=None, waits=()):
        qsem = {"y": "DY", "g": "DG"}[e]
        return self.op(e, fn, inc=qsem, delta=16, waits=waits, drain=False)

    def wait(self, e, tok):
        if tok is not None:
            self.q[e].append(("w", tok[0], tok[1]))


def build_program(n_seq=BPC, ngrp=NGRP_SEQ, debug=False, checkmode=False, serial=True,
                  serial_engines="tvsgy"):
    nc = bass.Bass("TRN2", target_bir_lowering=False, debug=False,
                   detect_race_conditions=checkmode)

    x_h = nc.dram_tensor("x", [BPC, SEQ, D], F32, kind="ExternalInput")
    out_h = nc.dram_tensor("out", [BPC, SEQ, D], F32, kind="ExternalOutput")
    w_in_h = nc.dram_tensor("w_in", [D, 2096], F32, kind="ExternalInput")
    b_in_h = nc.dram_tensor("b_in", [2096], F32, kind="ExternalInput")
    a_log_h = nc.dram_tensor("a_log", [16], F32, kind="ExternalInput")
    w_out_h = nc.dram_tensor("w_out", [D, D], F32, kind="ExternalInput")
    b_out_h = nc.dram_tensor("b_out", [D], F32, kind="ExternalInput")
    wsi_h = nc.dram_tensor("wsi", [D, 2048], F32, kind="ExternalInput")
    wso_h = nc.dram_tensor("wso", [D, D], F32, kind="ExternalInput")
    w1_h = nc.dram_tensor("w1", [D, 2048], F32, kind="ExternalInput")
    w2_h = nc.dram_tensor("w2", [8, D, D], F32, kind="ExternalInput")
    wg_h = nc.dram_tensor("wg", [D, 8], F32, kind="ExternalInput")
    gb_h = nc.dram_tensor("gb", [8], F32, kind="ExternalInput")
    idbf_h = nc.dram_tensor("idbf", [128, 128], BF16, kind="ExternalInput")
    idf32_h = nc.dram_tensor("idf32", [128, 128], F32, kind="ExternalInput")
    ones16_h = nc.dram_tensor("ones16", [16, 1], F32, kind="ExternalInput")

    x_a = x_h.ap()
    out_a = out_h.ap()

    sb = {}

    def S(name, shape, dtype):
        sb[name] = nc.alloc_sbuf_tensor(name, shape, dtype)
        return sb[name]

    W_in = S("W_in_sb", [128, KCH * PCOLS], BF16)
    W_out = S("W_out_sb", [128, KCH * D], BF16)
    Wso = S("Wso_sb", [128, KCH * D], BF16)
    Wsi = S("Wsi_sb", [128, KCH * 2048], BF16)
    W1 = S("W1_sb", [128, KCH * 2048], BF16)
    Wg = S("Wg_sb", [128, KCH * 8], BF16)
    w2b = [S(f"w2buf{i}", [128, KCH * 256], BF16) for i in range(2)]

    b_z = S("b_z", [128, D], BF16)
    b_sc = S("b_sc", [128, 48], F32)
    b_ot = S("b_ot", [128, D], BF16)
    gb_t = S("gb_t", [128, 8], F32)
    I_bf = S("I_bf", [128, 128], BF16)
    I_f32 = S("I_f32", [128, 128], F32)
    ones16 = S("ones16_sb", [16, 1], F32)
    A_raw = S("A_raw", [16, 1], F32)
    A_neg = S("A_neg", [16, 1], F32)
    one_c = S("one_c", [128, 1], F32)
    eps_c = S("eps_c", [128, 1], F32)
    mh_c = S("mh_c", [128, 1], F32)
    inv_c = S("inv_c", [128, 1], F32)
    mone_c = S("mone_c", [128, 1], F32)
    fl_y = S("fl_y", [128, 2], F32)
    fl_g = S("fl_g", [128, 2], BF16)
    xsig = S("xsig", [128, 128], F32)

    x_m1 = [S(f"x_m1_{i}", [128, D], F32) for i in range(2)]
    x_m2 = [S(f"x_m2_{i}", [128, D], F32) for i in range(1)]
    xsq = S("xsq", [128, D], F32)
    ln1 = S("ln1", [128, D], BF16)
    ln2 = S("ln2", [128, D], BF16)
    lnTa = S("lnTa", [128, D], BF16)
    lnTb = S("lnTb", [128, D], BF16)
    zb = S("zb", [128, D], BF16)
    gz = [S(f"gz{i}", [128, D], BF16) for i in range(1)]
    t48 = [S(f"t48_{i}", [128, 48], F32) for i in range(2)]
    attn_pre = [S(f"attn_pre{i}", [128, D], BF16) for i in range(GROUP)]
    hidT = [S(f"hidT{i}", [128, D], BF16) for i in range(GROUP)]
    acc = [S(f"acc{i}", [128, D], F32) for i in range(GROUP)]
    silu_a = S("silu_a", [128, D], BF16)
    b_cp = S("b_cp", [128, D], BF16)
    g1 = S("g1", [128, D], BF16)
    hid = S("hid", [128, D], BF16)

    SLB = S("SLB", [16, GROUP * 128], F32)
    SLC = S("SLC", [16, GROUP * 128], F32)
    SLD = S("SLD", [16, GROUP * 128], F32)
    s_a = S("s_a", [16, GROUP * 128], F32)
    s_s = S("s_s", [16, GROUP * 128], F32)
    carry = S("carry", [16, 1], F32)
    y_k = [S(f"y_k{i}", [128, 1], F32) for i in range(GROUP)]

    s1 = S("s1", [128, 1], F32)
    s2 = S("s2", [128, 1], F32)
    mu = S("mu", [128, 1], F32)
    msq = S("msq", [128, 1], F32)
    var = S("var", [128, 1], F32)
    lnv = S("lnv", [128, 1], F32)
    rstd = S("rstd", [128, 1], F32)

    lgt = S("lgt", [128, 8], F32)
    nmx = S("nmx", [128, 1], F32)
    exl = S("exl", [128, 8], F32)
    ssum = S("ssum", [128, 1], F32)
    rsum = S("rsum", [128, 1], F32)
    prb = S("prb", [128, 8], F32)
    m1s = S("m1s", [128, 1], F32)
    i1m = S("i1m", [128, 8], F32)
    ztmp = S("ztmp", [128, 8], F32)
    ztm2 = S("ztm2", [128, 8], F32)
    m2s = S("m2s", [128, 1], F32)
    wmask = S("wmask", [128, 8], F32)
    wd8 = S("wd8", [128, 8], F32)
    wd_e = [[S(f"wd_t{t}_e{e}", [128, 1], F32) for e in range(8)] for t in range(GROUP)]

    B = [nc.alloc_psum_tensor(f"pb{i}", [128, 512], F32) for i in range(8)]

    P = Planner()
    if serial:
        P.serial_engines = set(serial_engines)
        import os as _os4
        sp = _os4.environ.get("SPHASE", "m1a")
        if sp:
            P.serial_phases = set(sp.split(","))

    def dma_cast(dst_ap, src_ap):
        return lambda g: g.dma_start(out=dst_ap, in_=src_ap)

    def flush(queue_engine, srct):
        tgt = fl_y if queue_engine == "y" else fl_g
        return P.dma(queue_engine, (lambda t2, s2: (lambda e: e.dma_start(
            out=t2[:, 0:1], in_=s2[:, 0:1])))(tgt, srct))

    tok_w = None
    for j in range(KCH):
        tok_w = P.dma("g", dma_cast(W_in[:, j * PCOLS:(j + 1) * PCOLS],
                                    w_in_h.ap()[128 * j:128 * (j + 1), 1024:2096]))
        tok_w = P.dma("g", dma_cast(W_out[:, j * D:(j + 1) * D],
                                    w_out_h.ap()[128 * j:128 * (j + 1), :]))
        tok_w = P.dma("g", dma_cast(Wso[:, j * D:(j + 1) * D],
                                    wso_h.ap()[128 * j:128 * (j + 1), :]))
        tok_w = P.dma("g", dma_cast(Wsi[:, j * 2048:(j + 1) * 2048],
                                    wsi_h.ap()[128 * j:128 * (j + 1), :]))
        tok_w = P.dma("g", dma_cast(W1[:, j * 2048:(j + 1) * 2048],
                                    w1_h.ap()[128 * j:128 * (j + 1), :]))
        tok_w = P.dma("g", dma_cast(Wg[:, j * 8:(j + 1) * 8],
                                    wg_h.ap()[128 * j:128 * (j + 1), :]))

    tok_w = P.dma("g", dma_cast(b_z[:, :], b_in_h.ap()[1024:2048][None, :].to_broadcast([128, D])))
    tok_c = None
    for dst, srcap in [
        (b_sc[:, :], b_in_h.ap()[2048:2096][None, :].to_broadcast([128, 48])),
        (gb_t[:, :], gb_h.ap()[None, :].to_broadcast([128, 8])),
        (I_bf[:, :], idbf_h.ap()[:, :]),
        (I_f32[:, :], idf32_h.ap()[:, :]),
        (ones16[:, :], ones16_h.ap()[:, :]),
        (A_raw[:, :], a_log_h.ap()[:, None]),
    ]:
        tok_c = P.dma("y", (lambda d, s: (lambda e: e.dma_start(out=d, in_=s)))(dst, srcap))
    tok_w = P.dma("g", dma_cast(b_ot[:, :], b_out_h.ap()[None, :].to_broadcast([128, D])))
    tok_w = P.dma("g", dma_cast(b_ot[:, 1016:1024],
                                b_out_h.ap()[1016:1024][None, :].to_broadcast([128, 8])))

    tok_c = P.dma("y", (lambda e: e.dma_start(
        out=gb_t[:, :], in_=gb_h.ap()[None, :].to_broadcast([128, 8]))))
    P.op("v", lambda v: v.memset(one_c[:, :], 1.0))
    P.op("v", lambda v: v.memset(eps_c[:, :], 1e-5))
    P.op("v", lambda v: v.memset(mh_c[:, :], -0.5))
    P.op("v", lambda v: v.memset(inv_c[:, :], 1.0 / D))
    tok_msets = P.op("v", lambda v: v.memset(mone_c[:, :], -1.0), inc="V")

    tok_a1 = P.op("s", lambda s: s.activation(out=A_neg[:, :], in_=A_raw[:, :], func=AF.Exp),
                  inc="S", waits=[tok_c])
    P.wait("v", tok_a1)
    P.wait("v", tok_msets)
    tok_aneg = P.op("v", lambda v: v.tensor_scalar_mul(out=A_neg[:, :], in0=A_neg[:, :],
                                                       scalar1=mone_c[0:16, :]), inc="V")

    tokens_resident = tok_w
    tok_consts = tok_c

    last_T67 = None
    last_zbank = None
    last_w34 = None
    last_sc5 = None
    last_sc2 = None
    x1_free = [None, None]
    x2_free = [None]
    zb_free = None
    gz_free = [None]
    t48_free = [None, None]
    ln1_free = None
    ln2_free = None
    lnTa_free = None
    lnTb_free = None
    sil_free = None
    g1_free = None
    hid_free = None
    sl_free = None
    acc_free = [None] * GROUP
    hidT_free = [None] * GROUP
    ap_free = [None] * GROUP
    prev_scan = None
    prev_sa = None
    w2_free = [None, None]
    w2_loaded = [None, None]
    exp_bank_free = [None, None]

    store_toks = []

    def transpose_set(src_bf16, n_chunks, waits):
        tok = None
        for j in range(n_chunks):
            bank = B[6] if j < 4 else B[7]
            col = (j % 4) * 128
            fn = (lambda bk, c, jj: (lambda t: t.matmul(
                out=bk[:, c:c + 128], lhsT=src_bf16[:, jj * 128:(jj + 1) * 128],
                rhs=I_bf[:, :], start=True, stop=True)))(bank, col, j)
            tok = P.op("t", fn, inc="T" if j == n_chunks - 1 else None,
                       waits=waits if j == 0 else ())
        return tok

    def evac_T(dst_bf16, n_chunks, waits):
        P.op("s", (lambda d: (lambda s: s.copy(out=d[:, 0:512], in_=B[6][:, :])))(dst_bf16),
             waits=waits)
        t2 = P.op("s", (lambda d: (lambda s: s.copy(out=d[:, 512:1024], in_=B[7][:, :])))(dst_bf16),
                  inc="S")
        return t2

    def layernorm(x_sb, out_bf, extra_wait=None, sig=None):
        tv = P.op("v", lambda v: v.reduce_sum(out=s1[:, :], in_=x_sb[:, :], axis=AX.X),
                  waits=[extra_wait] if extra_wait else ())
        if sig is not None:
            P.op("v", (lambda c: (lambda v: v.tensor_copy(out=xsig[:, c:c + 1], in_=s1[:, :])))(sig))
        P.op("v", lambda v: v.tensor_scalar_mul(out=mu[:, :], in0=s1[:, :], scalar1=inv_c[:, :]))
        tmu = P.op("v", lambda v: v.tensor_mul(out=msq[:, :], in0=mu[:, :], in1=mu[:, :]), inc="V")
        tsq = P.op("s", lambda s: s.activation(out=xsq[:, :], in_=x_sb[:, :], func=AF.Square),
                   inc="S", waits=[extra_wait] if extra_wait else ())
        P.wait("v", tsq)
        P.op("v", lambda v: v.reduce_sum(out=s2[:, :], in_=xsq[:, :], axis=AX.X))
        tvar = P.op("v", lambda v: v.tensor_scalar(out=var[:, :], in0=s2[:, :],
                                                   scalar1=inv_c[:, :], scalar2=msq[:, :],
                                                   op0=ALU.mult, op1=ALU.subtract), inc="V")
        P.wait("s", tvar)
        P.op("s", lambda s: s.activation(out=lnv[:, :], in_=var[:, :], func=AF.Ln, bias=eps_c[:, :]))
        trs = P.op("s", lambda s: s.activation(out=rstd[:, :], in_=lnv[:, :], func=AF.Exp,
                                               scale=mh_c[:, :]), inc="S")
        P.wait("v", trs)
        tl = P.op("v", lambda v: v.tensor_scalar(out=out_bf[:, :], in0=x_sb[:, :],
                                                 scalar1=mu[:, :], scalar2=rstd[:, :],
                                                 op0=ALU.subtract, op1=ALU.mult), inc="V")
        return tl

    import os as _os2
    ydump_h = None
    if _os2.environ.get("YDUMP") == "1":
        ydump_h = nc.dram_tensor("ydump", [n_seq * ngrp, 128, GROUP], F32, kind="ExternalOutput")
        sdump_h = nc.dram_tensor("sdump", [n_seq * ngrp, 4, 16, GROUP * 128], F32, kind="ExternalOutput")
    first_tile = True
    for sidx in range(n_seq):
        for gidx in range(ngrp):
            g0 = gidx * GROUP
            sl_toks = []
            ap_toks = []
            for ti in range(GROUP):
                k = g0 + ti
                P.phase = "m1a"
                xbuf = x_m1[ti % 2]
                row0 = k * 128
                P.dma("y", (lambda xb, si, r0: (lambda e: e.dma_start(
                    out=xb[:, :], in_=x_a[si, r0:r0 + 128, :])))(xbuf, sidx, row0),
                    waits=[x1_free[ti % 2]])
                tok_x = P.dma("y", (lambda xb, si, r0: (lambda e: e.dma_start(
                    out=xb[:, 1016:1024], in_=x_a[si, r0:r0 + 128, 1016:1024])))(xbuf, sidx, row0))
                P.wait("v", tok_x)
                P.wait("s", tok_x)
                if first_tile:
                    P.wait("v", tokens_resident)
                    P.wait("v", tok_consts)
                    P.wait("v", tok_aneg)
                gtile = (sidx * ngrp + gidx) * GROUP + ti
                tok_ln1 = layernorm(xbuf, ln1, extra_wait=ln1_free,
                                    sig=(2 * gtile) % 128)
                x1_free[ti % 2] = tok_ln1
                P.phase = "m1b"
                tp_waits = [tok_ln1, last_T67]
                if sidx == 0 and gidx == 0 and ti == 0:
                    tp_waits.append(tok_consts)
                tok_tp = transpose_set(ln1, KCH, waits=tp_waits)
                tok_ev1 = evac_T(lnTa, KCH, waits=[tok_tp, lnTa_free])
                ln1_free = tok_tp
                P.phase = "m1c"
                wchain = [tok_ev1, last_zbank, last_sc2]
                if first_tile:
                    wchain.append(tokens_resident)
                tok_xz = None
                for (c0, c1, bank, bcol) in ((0, 512, B[0], 0), (512, 1024, B[1], 0),
                                             (1024, 1072, B[2], 0)):
                    for j in range(KCH):
                        fn = (lambda jj, a0, a1, bk, bc: (lambda t: t.matmul(
                            out=bk[:, bc:bc + (a1 - a0)],
                            lhsT=lnTa[:, jj * 128:(jj + 1) * 128],
                            rhs=W_in[:, jj * PCOLS + a0: jj * PCOLS + a1],
                            start=(jj == 0), stop=(jj == KCH - 1))))(j, c0, c1, bank, bcol)
                        tok_xz = P.op("t", fn, inc="T" if (j == KCH - 1 and c0 == 1024) else None,
                                      waits=wchain if (j == 0 and c0 == 0) else ())
                P.phase = "m1d"
                P.wait("v", tok_xz)
                if zb_free is not None:
                    P.wait("v", zb_free)
                import os as _os
                if _os.environ.get("DBG_ZB_COPY") == "1":
                    P.op("v", lambda v: v.tensor_copy(out=zb[:, 0:512], in_=B[0][:, :]))
                    tok_zb = P.op("v", lambda v: v.tensor_copy(out=zb[:, 512:1024], in_=B[1][:, :]), inc="V")
                else:
                    P.op("v", lambda v: v.tensor_add(out=zb[:, 0:512], in0=B[0][:, :], in1=b_z[:, 0:512]))
                    tok_zb = P.op("v", lambda v: v.tensor_add(out=zb[:, 512:1024], in0=B[1][:, :],
                                                              in1=b_z[:, 512:1024]), inc="V")
                last_zbank = tok_zb
                tb = t48[ti % 2]
                tok_t48 = P.op("v", (lambda tbb: (lambda v: v.tensor_add(
                    out=tbb[:, :], in0=B[2][:, 0:48], in1=b_sc[:, :])))(tb), inc="V",
                    waits=[t48_free[ti % 2]])
                last_sc2 = tok_t48
                gzb = gz[0]
                P.wait("s", tok_zb)
                P.op("s", (lambda gg: (lambda s: s.activation(out=gg[:, 0:512], in_=zb[:, 0:512],
                                                              func=AF.Gelu)))(gzb),
                     waits=[gz_free[0]])
                tok_gz = P.op("s", (lambda gg: (lambda s: s.activation(
                    out=gg[:, 512:1024], in_=zb[:, 512:1024], func=AF.Gelu)))(gzb), inc="S")
                zb_free = tok_gz
                tok_sct = None
                for ci in range(3):
                    fn = (lambda tbb, cc: (lambda t: t.matmul(
                        out=B[5][0:16, cc * 128:(cc + 1) * 128],
                        lhsT=tbb[:, cc * 16:(cc + 1) * 16],
                        rhs=I_f32[:, :], start=True, stop=True)))(tb, ci)
                    tok_sct = P.op("t", fn, inc="T" if ci == 2 else None,
                                   waits=[tok_t48, last_sc5] if ci == 0 else ())
                t48_free[ti % 2] = tok_sct
                P.wait("v", tok_sct)
                if ti == 0:
                    P.wait("v", sl_free)
                    P.wait("v", prev_sa)
                col = ti * 128
                P.op("v", (lambda c: (lambda v: v.tensor_copy(out=SLB[:, c:c + 128],
                                                              in_=B[5][0:16, 0:128])))(col))
                P.op("v", (lambda c: (lambda v: v.tensor_copy(out=SLC[:, c:c + 128],
                                                              in_=B[5][0:16, 128:256])))(col))
                tok_sl = P.op("v", (lambda c: (lambda v: v.tensor_copy(out=SLD[:, c:c + 128],
                                                                       in_=B[5][0:16, 256:384])))(col),
                              inc="V")
                last_sc5 = tok_sl
                sl_toks.append(tok_sl)
                P.phase = "m1e"
                tok_gzT = transpose_set(gzb, KCH, waits=[tok_gz, tok_ev1])
                tok_ev2 = evac_T(lnTb, KCH, waits=[tok_gzT, lnTb_free])
                gz_free[0] = tok_gzT
                last_T67 = tok_ev2
                lnTa_free = tok_xz
                tok_wo = None
                for (c0, bank) in ((0, B[3]), (512, B[4])):
                    for j in range(KCH):
                        fn = (lambda jj, a0, bk: (lambda t: t.matmul(
                            out=bk[:, :], lhsT=lnTb[:, jj * 128:(jj + 1) * 128],
                            rhs=W_out[:, jj * D + a0: jj * D + a0 + 512],
                            start=(jj == 0), stop=(jj == KCH - 1))))(j, c0, bank)
                        tok_wo = P.op("t", fn, inc="T" if (j == KCH - 1 and c0 == 512) else None,
                                      waits=[tok_ev2, last_w34] if (j == 0 and c0 == 0) else ())
                lnTb_free = tok_wo
                apb = attn_pre[ti]
                P.wait("s", tok_wo)
                P.wait("s", ap_free[ti])
                P.op("s", (lambda ab: (lambda s: s.copy(out=ab[:, 0:512], in_=B[3][:, :])))(apb))
                tok_ap = P.op("s", (lambda ab: (lambda s: s.copy(out=ab[:, 512:1024],
                                                                 in_=B[4][:, :])))(apb), inc="S")
                last_w34 = tok_ap
                ap_toks.append(tok_ap)
                first_tile = False

            P.phase = "scan"
            for tk in sl_toks:
                P.wait("s", tk)
            P.wait("s", prev_scan)
            P.op("s", lambda s: s.activation(out=s_a[:, :], in_=SLD[:, :], func=AF.Exp))
            P.op("s", lambda s: s.activation(out=s_a[:, :], in_=s_a[:, :], func=AF.Ln,
                                             bias=one_c[0:16, :]))
            tok_sa = P.op("s", lambda s: s.activation(out=s_a[:, :], in_=s_a[:, :], func=AF.Exp,
                                                      scale=A_neg[:, :]), inc="S")
            P.wait("v", tok_sa)
            if gidx == 0:
                tok_scan = P.op("v", lambda v: v.tensor_tensor_scan(
                    out=s_s[:, :], data0=s_a[:, :], data1=SLB[:, :], initial=0.0,
                    op0=ALU.mult, op1=ALU.add), inc="V")
            else:
                tok_scan = P.op("v", lambda v: v.tensor_tensor_scan(
                    out=s_s[:, :], data0=s_a[:, :], data1=SLB[:, :], initial=carry[:, :],
                    op0=ALU.mult, op1=ALU.add), inc="V")
            prev_scan = tok_scan
            prev_sa = tok_sa
            P.op("v", lambda v: v.tensor_copy(out=carry[:, :], in_=s_s[:, GROUP * 128 - 1:GROUP * 128]))
            tok_cs = P.op("v", lambda v: v.tensor_mul(out=SLC[:, :], in0=SLC[:, :], in1=s_s[:, :]),
                          inc="V")
            sl_free = tok_cs
            P.wait("t", tok_cs)
            tok_y = None
            for ti in range(GROUP):
                fn = (lambda c: (lambda t: t.matmul(
                    out=B[2][:, 64 + c:65 + c], lhsT=SLC[:, c * 128:(c + 1) * 128],
                    rhs=ones16[:, :], start=True, stop=True)))(ti)
                tok_y = P.op("t", fn, inc="T" if ti == GROUP - 1 else None,
                             waits=[last_sc2] if ti == 0 else ())
            P.wait("v", tok_y)
            tok_ye = None
            for ti in range(GROUP):
                tok_ye = P.op("v", (lambda c: (lambda v: v.tensor_copy(
                    out=y_k[c][:, :], in_=B[2][:, 64 + c:65 + c])))(ti),
                    inc="V" if ti == GROUP - 1 else None)
            last_sc2 = tok_ye

            if ydump_h is not None:
                gi = sidx * ngrp + gidx
                for nm_i, tsr in ((0, SLB), (1, SLC), (2, SLD), (3, s_s)):
                    P.dma("y", (lambda g2, i2, t2: (lambda e: e.dma_start(
                        out=sdump_h.ap()[g2, i2, :, :], in_=t2[:, :])))(gi, nm_i, tsr),
                        waits=[tok_cs, tok_sa])
                for ti in range(GROUP):
                    P.dma("y", (lambda g2, t2: (lambda e: e.dma_start(
                        out=ydump_h.ap()[g2, :, t2:t2 + 1], in_=y_k[t2][:, :],
                        allow_slow_non_contiguous=True)))(gi, ti), waits=[tok_ye])
            P.phase = "m2"
            sha_toks = []
            hid_toks = []
            wd_toks = []
            for ti in range(GROUP):
                k = g0 + ti
                xbuf = x_m2[0]
                row0 = k * 128
                P.dma("y", (lambda xb, si, r0: (lambda e: e.dma_start(
                    out=xb[:, :], in_=x_a[si, r0:r0 + 128, :])))(xbuf, sidx, row0),
                    waits=[x2_free[0]])
                tok_x2 = P.dma("y", (lambda xb, si, r0: (lambda e: e.dma_start(
                    out=xb[:, 1016:1024], in_=x_a[si, r0:r0 + 128, 1016:1024])))(xbuf, sidx, row0))
                ab = acc[ti]
                gtile2 = (sidx * ngrp + gidx) * GROUP + ti
                P.wait("v", tok_x2)
                P.wait("v", tok_ye)
                P.wait("v", ap_toks[ti])
                P.wait("v", acc_free[ti])
                P.op("v", (lambda aa, apb2, yk, xb: (lambda v: v.scalar_tensor_tensor(
                    out=aa[:, :], in0=apb2[:, :], scalar=yk[:, :], in1=xb[:, :],
                    op0=ALU.mult, op1=ALU.add)))(ab, attn_pre[ti], y_k[ti], xbuf))
                tok_seed = P.op("v", (lambda aa: (lambda v: v.tensor_add(
                    out=aa[:, :], in0=aa[:, :], in1=b_ot[:, :])))(ab), inc="V")
                x2_free[0] = tok_seed
                ap_free[ti] = tok_seed
                P.wait("s", tok_seed)
                tok_ln2 = layernorm(ab, ln2, extra_wait=ln2_free,
                                    sig=(2 * gtile2 + 1) % 128)
                tok_tp2 = transpose_set(ln2, KCH, waits=[tok_ln2, last_T67])
                tok_ev3 = evac_T(lnTa, KCH, waits=[tok_tp2, lnTa_free])
                ln2_free = tok_tp2
                tok_shin = None
                for ci in range(4):
                    for j in range(KCH):
                        fn = (lambda jj, cc: (lambda t: t.matmul(
                            out=B[cc][:, :], lhsT=lnTa[:, jj * 128:(jj + 1) * 128],
                            rhs=Wsi[:, jj * 2048 + cc * 512: jj * 2048 + (cc + 1) * 512],
                            start=(jj == 0), stop=(jj == KCH - 1))))(j, ci)
                        tok_shin = P.op("t", fn,
                                        inc="T" if (j == KCH - 1 and ci == 3) else None,
                                        waits=[tok_ev3, last_zbank, last_sc2]
                                        if (j == 0 and ci == 0) else ())
                tok_gate = None
                for j in range(KCH):
                    fn = (lambda jj: (lambda t: t.matmul(
                        out=B[4][:, 0:8], lhsT=lnTa[:, jj * 128:(jj + 1) * 128],
                        rhs=Wg[:, jj * 8:(jj + 1) * 8],
                        start=(jj == 0), stop=(jj == KCH - 1))))(j)
                    tok_gate = P.op("t", fn, inc="T" if j == KCH - 1 else None,
                                    waits=[last_w34] if j == 0 else ())
                P.wait("v", tok_gate)
                P.op("v", lambda v: v.tensor_add(out=lgt[:, :], in0=B[4][:, 0:8], in1=gb_t[:, :]))
                tok_lg = P.op("v", lambda v: v.reduce_max(out=nmx[:, :], in_=lgt[:, :],
                                                          axis=AX.X, negate=True), inc="V")
                P.wait("s", tok_lg)
                tok_ex = P.op("s", lambda s: s.activation(out=exl[:, :], in_=lgt[:, :],
                                                          func=AF.Exp, bias=nmx[:, :]), inc="S")
                P.wait("v", tok_ex)
                P.op("v", lambda v: v.reduce_sum(out=ssum[:, :], in_=exl[:, :], axis=AX.X))
                P.op("v", lambda v: v.reciprocal(out=rsum[:, :], in_=ssum[:, :]))
                P.op("v", lambda v: v.tensor_scalar_mul(out=prb[:, :], in0=exl[:, :],
                                                        scalar1=rsum[:, :]))
                P.op("v", lambda v: v.reduce_max(out=m1s[:, :], in_=prb[:, :], axis=AX.X))
                P.op("v", lambda v: v.tensor_single_scalar(out=i1m[:, :], in_=prb[:, :],
                                                           scalar=m1s[:, :], op=ALU.is_ge))
                P.op("v", lambda v: v.tensor_mul(out=ztmp[:, :], in0=prb[:, :], in1=i1m[:, :]))
                P.op("v", lambda v: v.tensor_sub(out=ztm2[:, :], in0=prb[:, :], in1=ztmp[:, :]))
                P.op("v", lambda v: v.reduce_max(out=m2s[:, :], in_=ztm2[:, :], axis=AX.X))
                P.op("v", lambda v: v.tensor_single_scalar(out=wmask[:, :], in_=prb[:, :],
                                                           scalar=m2s[:, :], op=ALU.is_ge))
                P.op("v", lambda v: v.tensor_mul(out=wd8[:, :], in0=prb[:, :], in1=wmask[:, :]))
                tok_wd = None
                for e in range(8):
                    tok_wd = P.op("v", (lambda tt, ee: (lambda v: v.tensor_copy(
                        out=wd_e[tt][ee][:, :], in_=wd8[:, ee:ee + 1])))(ti, e),
                        inc="V" if e == 7 else None)
                wd_toks.append(tok_wd)
                P.wait("s", tok_shin)
                P.op("s", lambda s: s.activation(out=silu_a[:, 0:512], in_=B[0][:, :], func=AF.Silu),
                     waits=[sil_free])
                P.op("s", lambda s: s.activation(out=silu_a[:, 512:1024], in_=B[1][:, :], func=AF.Silu))
                P.op("s", lambda s: s.copy(out=b_cp[:, 0:512], in_=B[2][:, :]))
                tok_sw1 = P.op("s", lambda s: s.copy(out=b_cp[:, 512:1024], in_=B[3][:, :]), inc="S")
                P.wait("v", tok_sw1)
                tok_g1 = P.op("v", lambda v: v.tensor_mul(out=g1[:, :], in0=silu_a[:, :],
                                                          in1=b_cp[:, :]), inc="V",
                              waits=[g1_free])
                tok_g1T = transpose_set(g1, KCH, waits=[tok_g1, last_T67])
                tok_ev4 = evac_T(lnTb, KCH, waits=[tok_g1T, lnTb_free])
                g1_free = tok_g1T
                tok_sho = None
                for (c0, bank) in ((0, B[4]), (512, B[5])):
                    for j in range(KCH):
                        fn = (lambda jj, a0, bk: (lambda t: t.matmul(
                            out=bk[:, :], lhsT=lnTb[:, jj * 128:(jj + 1) * 128],
                            rhs=Wso[:, jj * D + a0: jj * D + a0 + 512],
                            start=(jj == 0), stop=(jj == KCH - 1))))(j, c0, bank)
                        tok_sho = P.op("t", fn, inc="T" if (j == KCH - 1 and c0 == 512) else None,
                                       waits=[tok_ev4, tok_lg, last_sc5] if (j == 0 and c0 == 0) else ())
                lnTb_free = tok_sho
                last_sc5 = tok_sho
                tok_w1 = None
                for ci in range(4):
                    for j in range(KCH):
                        fn = (lambda jj, cc: (lambda t: t.matmul(
                            out=B[cc][:, :], lhsT=lnTa[:, jj * 128:(jj + 1) * 128],
                            rhs=W1[:, jj * 2048 + cc * 512: jj * 2048 + (cc + 1) * 512],
                            start=(jj == 0), stop=(jj == KCH - 1))))(j, ci)
                        tok_w1 = P.op("t", fn, inc="T" if (j == KCH - 1 and ci == 3) else None,
                                      waits=[tok_sw1] if (j == 0 and ci == 0) else ())
                lnTa_free = tok_w1
                P.wait("s", tok_w1)
                P.op("s", lambda s: s.activation(out=silu_a[:, 0:512], in_=B[0][:, :], func=AF.Silu))
                P.op("s", lambda s: s.activation(out=silu_a[:, 512:1024], in_=B[1][:, :], func=AF.Silu))
                P.op("s", lambda s: s.copy(out=b_cp[:, 0:512], in_=B[2][:, :]))
                tok_sw2 = P.op("s", lambda s: s.copy(out=b_cp[:, 512:1024], in_=B[3][:, :]), inc="S")
                sil_free = tok_sw2
                P.wait("v", tok_sw2)
                tok_hid = P.op("v", lambda v: v.tensor_mul(out=hid[:, :], in0=silu_a[:, :],
                                                           in1=b_cp[:, :]), inc="V",
                               waits=[hid_free])
                last_zbank = tok_hid
                tok_hT = transpose_set(hid, KCH, waits=[tok_hid, last_T67])
                tok_ev5 = evac_T(hidT[ti], KCH, waits=[tok_hT, hidT_free[ti]])
                hid_free = tok_hT
                last_T67 = tok_ev5
                hid_toks.append(tok_ev5)
                P.wait("v", tok_sho)
                P.op("v", (lambda aa: (lambda v: v.tensor_add(out=aa[:, 0:512], in0=aa[:, 0:512],
                                                              in1=B[4][:, :])))(ab))
                tok_sha = P.op("v", (lambda aa: (lambda v: v.tensor_add(
                    out=aa[:, 512:1024], in0=aa[:, 512:1024], in1=B[5][:, :])))(ab), inc="V")
                sha_toks.append(tok_sha)
                last_w34 = tok_sha
                last_sc5 = tok_sha

            P.phase = "exp"
            exp_acc_tok = list(sha_toks)
            exp_bank_free4 = [last_zbank, last_zbank, last_zbank, last_zbank]
            for e in range(8):
                for fb in range(2):
                    mm_toks = [None] * GROUP
                    for q2 in range(2):
                        qb = fb * 2 + q2
                        pi = (e * 4 + qb) % 2
                        c0 = qb * 256
                        srcap = w2_h.ap()[e, :, c0:c0 + 256].rearrange(
                            "(j p) c -> p j c", j=KCH, p=128)
                        dst = w2b[pi][:, :].rearrange("p (j c) -> p j c", j=KCH, c=256)
                        P.dma("g", (lambda d, s: (lambda g: g.dma_start(
                            out=d, in_=s)))(dst, srcap), waits=[w2_free[pi]])
                        src2 = w2_h.ap()[e, :, c0 + 248:c0 + 256].rearrange(
                            "(j p) c -> p j c", j=KCH, p=128)
                        dst2 = w2b[pi][:, :].rearrange(
                            "p (j c) -> p j c", j=KCH, c=256)[:, :, 248:256]
                        tok_wl = P.dma("g", (lambda d, s: (lambda g: g.dma_start(
                            out=d, in_=s)))(dst2, src2))
                        last_mm = None
                        for ti in range(GROUP):
                            bk = B[ti]
                            tok_mm = None
                            for j in range(KCH):
                                fn = (lambda jj, tt, bkk, ppi, qq2: (lambda t: t.matmul(
                                    out=bkk[:, qq2 * 256:(qq2 + 1) * 256],
                                    lhsT=hidT[tt][:, jj * 128:(jj + 1) * 128],
                                    rhs=w2b[ppi][:, jj * 256:(jj + 1) * 256],
                                    start=(jj == 0), stop=(jj == KCH - 1))))(j, ti, bk, pi, q2)
                                w = ()
                                if j == 0:
                                    w = [tok_wl, hid_toks[ti],
                                         exp_bank_free4[ti] if q2 == 0 else None]
                                tok_mm = P.op("t", fn, inc="T" if j == KCH - 1 else None, waits=w)
                            last_mm = tok_mm
                            mm_toks[ti] = tok_mm
                        w2_free[pi] = last_mm
                        if e == 7 and fb == 1 and q2 == 1:
                            for tt2 in range(GROUP):
                                hidT_free[tt2] = mm_toks[tt2]
                    for ti in range(GROUP):
                        P.wait("v", mm_toks[ti])
                        P.wait("v", wd_toks[ti])
                        P.wait("v", exp_acc_tok[ti])
                        tok_st = P.op("v", (lambda aa, bkk, sc, f0: (lambda v: v.scalar_tensor_tensor(
                            out=aa[:, f0:f0 + 512], in0=bkk[:, :], scalar=sc[:, :],
                            in1=aa[:, f0:f0 + 512], op0=ALU.mult, op1=ALU.add)))(
                                acc[ti], B[ti], wd_e[ti][e], fb * 512), inc="V")
                        exp_acc_tok[ti] = tok_st
                        exp_bank_free4[ti] = tok_st
            last_zbank = exp_acc_tok[GROUP - 1]
            P.phase = "store"
            for ti in range(GROUP):
                k = g0 + ti
                row0 = k * 128
                tok_store = P.dma("y", (lambda aa, si, r0: (lambda e: e.dma_start(
                    out=out_a[si, r0:r0 + 128, :], in_=aa[:, :])))(acc[ti], sidx, row0),
                    waits=[exp_acc_tok[ti]])
                acc_free[ti] = tok_store
                store_toks.append(tok_store)

    xsig_h = nc.dram_tensor("xsig_out", [128, 128], F32, kind="ExternalOutput")
    P.q["v"].append(("w", "DY", P.cnt["DY"]))
    P.dma("y", lambda e: e.dma_start(out=xsig_h.ap()[:, :], in_=xsig[:, :]),
          waits=[("V", P.cnt.get("V", 0))])
    if debug:
        dbg_specs = [
            ("ln1", ln1), ("zb", zb), ("gz", gz[0]), ("SLB", SLB), ("SLC", SLC),
            ("SLD", SLD), ("s_a", s_a), ("s_s", s_s),
            ("ap0", attn_pre[0]), ("ap3", attn_pre[3]),
            ("ln2", ln2), ("g1", g1), ("hid", hid), ("wd8", wd8), ("prb", prb),
            ("lnTa", lnTa), ("lnTb", lnTb), ("hidT0", hidT[0]),
            ("y0", y_k[0]), ("y3", y_k[3]), ("rstd", rstd), ("mu", mu),
            ("b_z", b_z), ("b_ot", b_ot), ("acc0", acc[0]), ("Win0", W_in),
            ("carry", carry), ("exl", exl), ("lgt", lgt),
        ]
        for nm in ("V", "S", "T"):
            if nm in P.cnt:
                P.q["y"].append(("w", nm, P.cnt[nm]))
        P.wait("y", store_toks[-1])
        for nm, t in dbg_specs:
            h = nc.dram_tensor("dbg_" + nm, list(t.shape), t.dtype, kind="ExternalOutput")
            P.dma("y", (lambda hh, tt: (lambda e: e.dma_start(out=hh.ap()[:, :], in_=tt[:, :])))(h, t))
    P.q["y"].append(("w", "DY", P.cnt["DY"]))

    sem_names = sorted(P.cnt.keys())
    sems = {}
    import contextlib
    stack = contextlib.ExitStack()
    for n in sem_names:
        sems[n] = stack.enter_context(nc.semaphore(n))

    self_sems = {}
    if checkmode:
        for e in ("t", "v", "s", "g", "y"):
            self_sems[e] = stack.enter_context(nc.semaphore("self_" + e))
    gl_sem = stack.enter_context(nc.semaphore("GL")) if serial else None

    self_dma_cnt = {}

    def run_queue(ename, eng, q):
        selfcnt = 0
        pending_gidx = None
        for item in q:
            if item[0] == "g":
                pending_gidx = item[1]
                continue
            if item[0] == "w":
                eng.wait_ge(sems[item[1]], item[2])
            elif item[0] == "d":
                if not checkmode:
                    eng.drain()
            else:
                _, fn, inc, delta, val = item
                if checkmode and selfcnt:
                    eng.wait_ge(self_sems[ename], selfcnt)
                if checkmode and inc in ("DY", "DG"):
                    prev = self_dma_cnt.get(inc, 0)
                    if prev:
                        eng.wait_ge(sems[inc], prev)
                    self_dma_cnt[inc] = prev + 16
                if serial and pending_gidx:
                    eng.wait_ge(gl_sem, pending_gidx)
                ins = fn(eng)
                if inc is not None:
                    ins.then_inc(sems[inc], delta)
                if serial and pending_gidx is not None:
                    if inc in ("DY", "DG"):
                        eng.wait_ge(sems[inc], val)
                    elif ename in ("v", "s", "g", "t"):
                        eng.drain()
                    eng.nop().then_inc(gl_sem, 1)
                    pending_gidx = None
                if checkmode:
                    step = 16 if ename in ("y", "g") else 1
                    selfcnt += step
                    eng.nop().then_inc(self_sems[ename], step)

    with nc.Block() as block:
        @block.sync
        def _(sync):
            run_queue("y", sync, P.q["y"])

        @block.gpsimd
        def _(gpsimd):
            run_queue("g", gpsimd, P.q["g"])

        @block.vector
        def _(vector):
            run_queue("v", vector, P.q["v"])

        @block.scalar
        def _(scalar):
            run_queue("s", scalar, P.q["s"])

        @block.tensor
        def _(tensor):
            run_queue("t", tensor, P.q["t"])

    stack.close()
    return nc


def make_inputs_for_core(core, full):
    x = np.ascontiguousarray(full["L0_attn_x"][core * BPC:(core + 1) * BPC])
    return {
        "x": x,
        "w_in": full["L0_attn_W_in"],
        "b_in": full["L0_attn_b_in"],
        "a_log": np.ascontiguousarray(full["L0_attn_A_log"]),
        "w_out": full["L0_attn_W_out"],
        "b_out": full["L0_attn_b_out"],
        "wsi": full["L0_ffn_w_shared_in"],
        "wso": full["L0_ffn_w_shared_out"],
        "w1": full["L0_ffn_w1_shared"],
        "w2": full["L0_ffn_w2"],
        "wg": full["L0_ffn_w_gate"],
        "gb": full["L0_ffn_gate_bias"],
        "idbf": np.eye(128).astype(BF),
        "idf32": np.eye(128, dtype=np.float32),
        "ones16": np.ones((16, 1), np.float32),
    }


def kernel(**inputs):
    full = {k: np.asarray(v) for k, v in inputs.items()}
    nc = build_program()
    in_maps = [make_inputs_for_core(c, full) for c in range(NCORE)]
    res = run_bass_kernel_spmd(nc, in_maps, core_ids=list(range(NCORE)))
    out = np.concatenate([res.results[c]["out"] for c in range(NCORE)], axis=0)
    return out
